# version 01
# baseline (speedup 1.0000x reference)
import contextlib
import math
import numpy as np
import ml_dtypes
import concourse.bass as bass
import concourse.mybir as mybir
from concourse.alu_op_type import AluOpType
from concourse.bass_utils import run_bass_kernel_spmd

F32 = mybir.dt.float32
BF16 = mybir.dt.bfloat16
I8 = mybir.dt.int8
AF = mybir.ActivationFunctionType
AX = mybir.AxisListType

T = 2048
C = 2048
KC = 16
TT = 16
NH0 = 16
NKV = 4
SCALE = 1.0 / math.sqrt(128.0)
NCORES = 8
NEXP = 16
KSEL = 1024
BIS_ITERS = 28
BF = ml_dtypes.bfloat16

ENGINES = ("sync", "gpsimd", "tensor", "vector", "scalar")


class Prog:
    def __init__(self, nc, ctx):
        self.nc = nc
        self.ctx = ctx
        self.streams = {e: [] for e in ENGINES}
        self.sems = {e: ctx.enter_context(nc.semaphore(f"sem_{e}")) for e in ENGINES}
        self.cnt = {e: 0 for e in ENGINES}
        self.sem_st = ctx.enter_context(nc.semaphore("sem_st"))
        self.cnt_st = 0
        self.waited = {"sync": 0, "gpsimd": 0, "st": 0}
        self.flushed = {"sync": 0, "gpsimd": 0, "st": 0}

    def _dma_flush(self, eng, key):
        sem = self.sem_st if key == "st" else self.sems[key]
        if self.waited[key] > self.flushed[key]:
            b = self.waited[key]
            self.streams[eng].append(lambda e: e.wait_ge(sem, b))
            self.flushed[key] = b

    def tick(self, eng, fn):
        step = 16 if eng in ("sync", "gpsimd") else 1
        sem = self.sems[eng]
        if eng in ("vector", "scalar") and self.cnt[eng] > 0:
            prev = self.cnt[eng]
            self.streams[eng].append(lambda e: e.wait_ge(sem, prev))
        if eng in ("sync", "gpsimd"):
            self._dma_flush(eng, eng)
        self.streams[eng].append(lambda e: fn(e).then_inc(sem, step))
        self.cnt[eng] += step
        return self.cnt[eng]

    def ticka(self, eng, fn):
        return self.tick(eng, fn)

    def dma(self, eng, out, in_, **kw):
        return self.tick(eng, lambda e: e.dma_start(out=out, in_=in_, **kw))

    def dmat(self, out, in_):
        return self.tick("sync", lambda e: e.dma_start_transpose(out, in_))

    def store(self, out, in_):
        sem = self.sem_st
        self._dma_flush("sync", "st")
        self.cnt_st += 16
        self.streams["sync"].append(
            lambda e: e.dma_start(out=out, in_=in_).then_inc(sem, 16))
        return self.cnt_st

    def wait(self, eng, on_eng, count):
        if count <= 0:
            return
        if on_eng in ("sync", "gpsimd"):
            self.waited[on_eng] = max(self.waited[on_eng], count)
        sem = self.sems[on_eng]
        self.streams[eng].append(lambda e: e.wait_ge(sem, count))

    def barrier(self, engines=ENGINES):
        counts = {e: self.cnt[e] for e in engines}
        nst = self.cnt_st
        sem_st = self.sem_st
        for e in engines:
            for o in engines:
                if o != e:
                    self.wait(e, o, counts[o])
            if nst > 0:
                self.streams[e].append(lambda eh: eh.wait_ge(sem_st, nst))
        if nst > 0:
            self.waited["st"] = max(self.waited["st"], nst)
        for key, eng in (("sync", "sync"), ("gpsimd", "gpsimd")):
            if eng in engines:
                self.waited[key] = max(self.waited[key], counts[eng])
                self._dma_flush(eng, key)

    def finalize(self):
        sem = self.sem_st
        n = self.cnt_st
        self.streams["sync"].append(lambda e: e.wait_ge(sem, n))

    def build_block(self):
        block = self.ctx.enter_context(self.nc.Block())
        st = self.streams

        @block.sync
        def _(e):
            for fn in st["sync"]:
                fn(e)

        @block.gpsimd
        def _(e):
            for fn in st["gpsimd"]:
                fn(e)

        @block.tensor
        def _(e):
            for fn in st["tensor"]:
                fn(e)

        @block.vector
        def _(e):
            for fn in st["vector"]:
                fn(e)

        @block.scalar
        def _(e):
            for fn in st["scalar"]:
                fn(e)


def ts(i, n=128):
    return slice(i * n, (i + 1) * n)


def emit_bcast_row(p, sb, pB, row_ap, out_ap, n):
    for j in range((n + 511) // 512):
        w = min(512, n - j * 512)
        tm = p.tick("tensor", lambda e, j=j, w=w: e.matmul(
            pB[:, 0:w], sb["onesf"][0:1, :], row_ap[0:1, j * 512:j * 512 + w],
            start=True, stop=True))
        p.wait("vector", "tensor", tm)
        tv = p.tick("vector", lambda e, j=j, w=w: e.tensor_copy(
            out_ap[:, j * 512:j * 512 + w], pB[:, 0:w]))
        p.wait("tensor", "vector", tv)


def emit_quant_round(p, sb, x_ap, rq_ap, out_bf_ap):
    v = sb["work2"]
    r = sb["work3"]
    n8 = sb["i8tile"]
    v1 = p.tick("vector", lambda e: e.tensor_scalar(
        v[:], x_ap, rq_ap, None, AluOpType.mult))
    p.wait("scalar", "vector", v1)
    s1 = p.tick("scalar", lambda e: e.activation(n8[:], v[:], AF.Identity))
    p.wait("vector", "scalar", s1)
    p.tick("vector", lambda e: e.scalar_tensor_tensor(
        r[:], n8[:], -1.0, v[:], AluOpType.mult, AluOpType.add))
    p.tick("vector", lambda e: e.tensor_scalar(
        sb["work1"][:], r[:], 0.5, None, AluOpType.is_ge))
    p.tick("vector", lambda e: e.tensor_scalar(
        v[:], r[:], -0.5, None, AluOpType.is_lt))
    p.tick("vector", lambda e: e.tensor_tensor(
        r[:], n8[:], sb["work1"][:], AluOpType.add))
    vq = p.tick("vector", lambda e: e.tensor_tensor(
        out_bf_ap, r[:], v[:], AluOpType.subtract))
    return vq


def emit_col_to_rep(p, sb, pB, col_ap, out_ap, srow_ap):
    p.wait("tensor", "vector", p.cnt["vector"])
    p.wait("tensor", "scalar", p.cnt["scalar"])
    tm = p.tick("tensor", lambda e: e.transpose(pB[0:1, 0:128], col_ap, sb["identf"][:]))
    p.wait("vector", "tensor", tm)
    tv = p.tick("vector", lambda e: e.tensor_copy(srow_ap[0:1, 0:128], pB[0:1, 0:128]))
    p.wait("tensor", "vector", tv)
    tm2 = p.tick("tensor", lambda e: e.matmul(
        pB[:, 128:256], sb["onesf"][0:1, :], srow_ap[0:1, 0:128], start=True, stop=True))
    p.wait("vector", "tensor", tm2)
    tv2 = p.tick("vector", lambda e: e.tensor_copy(out_ap, pB[:, 128:256]))
    p.wait("tensor", "vector", tv2)


def emit_ln_stats(p, sb, x_ap, pref):
    st = sb[pref]
    trash = sb["work2"]
    v1 = p.tick("vector", lambda e: e.tensor_reduce(
        st[:, 0:1], x_ap, AX.X, AluOpType.add))
    s1 = p.ticka("scalar", lambda e: e.activation(
        trash[:], x_ap, AF.Square, accum_out=st[:, 3:4]))
    p.wait("vector", "scalar", s1)
    v2 = p.tick("vector", lambda e: e.tensor_scalar(
        st[:, 1:2], st[:, 0:1], 1.0 / C, None, AluOpType.mult))
    v3 = p.tick("vector", lambda e: e.tensor_scalar(
        st[:, 8:9], st[:, 3:4], 1.0 / C, None, AluOpType.mult))
    p.wait("scalar", "vector", v3)
    s2 = p.tick("scalar", lambda e: e.activation(st[:, 2:3], st[:, 1:2], AF.Square))
    p.wait("vector", "scalar", s2)
    v4 = p.tick("vector", lambda e: e.tensor_tensor(
        st[:, 4:5], st[:, 8:9], st[:, 2:3], AluOpType.subtract))
    v4b = p.tick("vector", lambda e: e.tensor_scalar(
        st[:, 9:10], st[:, 4:5], 1e-5, None, AluOpType.add))
    p.wait("scalar", "vector", v4b)
    s3 = p.tick("scalar", lambda e: e.activation(
        st[:, 5:6], st[:, 9:10], AF.Sqrt))
    p.wait("vector", "scalar", s3)
    v5 = p.tick("vector", lambda e: e.reciprocal(st[:, 6:7], st[:, 5:6]))
    v6 = p.tick("vector", lambda e: e.tensor_scalar(
        st[:, 7:8], st[:, 1:2], st[:, 6:7], -1.0, AluOpType.mult, AluOpType.mult))
    return st[:, 6:7], st[:, 7:8], v6


def emit_col_to_row(p, sb, pB, col_ap, slot):
    srow = sb["srow"]
    p.wait("tensor", "vector", p.cnt["vector"])
    p.wait("tensor", "scalar", p.cnt["scalar"])
    tm = p.tick("tensor", lambda e: e.transpose(pB[0:1, 0:128], col_ap, sb["identf"][:]))
    p.wait("vector", "tensor", tm)
    tv = p.tick("vector", lambda e: e.tensor_copy(
        srow[0:1, slot * 128:(slot + 1) * 128], pB[0:1, 0:128]))
    p.wait("tensor", "vector", tv)
    return tv


def emit_col_sumsq11(p, sb, pB, col_ap, out_off):
    srow = sb["srow"]
    tv = emit_col_to_row(p, sb, pB, col_ap, 3)
    p.wait("scalar", "vector", tv)
    ss = p.tick("scalar", lambda e: e.activation(
        srow[0:1, 256:384], srow[0:1, 384:512], AF.Square,
        accum_out=srow[0:1, out_off:out_off + 1]))
    p.wait("vector", "scalar", ss)
    p.wait("tensor", "scalar", ss)


def emit_col_max11(p, sb, pB, col_ap, out_off):
    srow = sb["srow"]
    tv = emit_col_to_row(p, sb, pB, col_ap, 3)
    vm = p.tick("vector", lambda e: e.tensor_reduce(
        srow[0:1, out_off:out_off + 1], srow[0:1, 384:512], AX.X, AluOpType.max))
    p.wait("tensor", "vector", vm)
    p.wait("scalar", "vector", vm)


def emit_negc_from_sq(p, sb, pB, ap_a, ap_b, out_negc):
    srow = sb["srow"]
    v1 = p.tick("vector", lambda e: e.tensor_tensor(
        srow[0:1, 130:131], ap_a, ap_b, AluOpType.mult))
    p.wait("scalar", "vector", v1)
    s1 = p.tick("scalar", lambda e: e.activation(
        srow[0:1, 131:132], srow[0:1, 130:131], AF.Sqrt))
    p.wait("vector", "scalar", s1)
    v2 = p.tick("vector", lambda e: e.tensor_scalar(
        srow[0:1, 132:133], srow[0:1, 131:132], -SCALE, 30.0,
        AluOpType.mult, AluOpType.add))
    p.wait("tensor", "vector", v2)
    tm2 = p.tick("tensor", lambda e: e.matmul(
        pB[:, 128:129], sb["onesf"][0:1, :], srow[0:1, 132:133], start=True, stop=True))
    p.wait("vector", "tensor", tm2)
    tv2 = p.tick("vector", lambda e: e.tensor_copy(out_negc, pB[:, 128:129]))
    p.wait("tensor", "vector", tv2)
    p.wait("scalar", "vector", tv2)


def emit_attention(p, sb, psums, kT_ap, ktm_ap, qT_ap, yT_store, negc_ap, nheads_note=None):
    pA, pB = psums
    for h in range(NH0 if nheads_note is None else nheads_note):
        for Q in range(4):
            jmax = (Q + 1) * 4
            for j in range(jmax):
                dj = j * 128 - Q * 512
                doff = max(dj, 0)
                w = 512 - doff
                pb = pB[:, (j % 2) * 512:(j % 2) * 512 + w]
                tqk = p.tick("tensor", lambda e, h=h, j=j, Q=Q, pb=pb, doff=doff, w=w: e.matmul(
                    pb, kT_ap(h, j), qT_ap(h, Q)[:, doff:doff + w], start=True, stop=True))
                p.wait("scalar", "tensor", tqk)
                eb = sb["expb0"][:, 0:w]
                se = p.tick("scalar", lambda e, pb=pb, eb=eb, h=h: e.activation(
                    eb, pb, AF.Exp, bias=negc_ap(h), scale=SCALE))
                if dj >= 0:
                    p.wait("vector", "scalar", se)
                    p.tick("vector", lambda e, eb=eb: e.tensor_tensor(
                        sb["expb1"][:], eb[:, 0:128], sb["triu"][:], AluOpType.mult))
                    cons = p.tick("vector", lambda e, eb=eb: e.tensor_copy(
                        eb[:, 0:128], sb["expb1"][:]))
                    p.wait("tensor", "vector", cons)
                    p.wait("tensor", "scalar", se)
                else:
                    p.wait("tensor", "scalar", se)
                tpv = p.tick("tensor", lambda e, h=h, j=j, jmax=jmax, eb=eb, doff=doff, w=w: e.matmul(
                    pA[:, doff:doff + w], ktm_ap(h, j), eb,
                    start=(j == 0), stop=(j == jmax - 1)))
                trs = p.tick("tensor", lambda e, j=j, jmax=jmax, eb=eb, doff=doff, w=w: e.matmul(
                    pA[:, 512 + doff:512 + doff + w], sb["ones128"][:], eb,
                    start=(j == 0), stop=(j == jmax - 1)))
                p.wait("scalar", "tensor", trs)
            sl = p.tick("scalar", lambda e: e.activation(
                sb["lntmp"][:, 0:512], pA[:, 512:1024], AF.Ln))
            sr = p.tick("scalar", lambda e: e.activation(
                sb["rrep"][:, 0:512], sb["lntmp"][:, 0:512], AF.Exp, scale=-1.0))
            p.wait("vector", "scalar", sr)
            vy = yT_store(h, Q, pA[:, 0:512], sb["rrep"][:, 0:512])
            p.wait("tensor", "vector", vy)
            p.wait("tensor", "scalar", sr)
            p.wait("scalar", "vector", vy)


def build_program_a(debug=False):
    nc = bass.Bass("TRN2", target_bir_lowering=False)
    ctx = contextlib.ExitStack()
    nc._ctx = ctx
    p = Prog(nc, ctx)

    x_in = nc.dram_tensor("x", [T, C], F32, kind="ExternalInput").ap()
    w_lat = nc.dram_tensor("w_lat", [C, 128], F32, kind="ExternalInput").ap()
    b_lat = nc.dram_tensor("b_lat", [128], F32, kind="ExternalInput").ap()
    w_d = nc.dram_tensor("w_d", [C, C], F32, kind="ExternalInput").ap()
    b_d = nc.dram_tensor("b_d", [C], F32, kind="ExternalInput").ap()
    w_proj = nc.dram_tensor("w_proj", [C, C], F32, kind="ExternalInput").ap()
    b_proj = nc.dram_tensor("b_proj", [C], F32, kind="ExternalInput").ap()
    ffn_w = nc.dram_tensor("ffn_w", [C, C], F32, kind="ExternalInput").ap()
    ffn_b = nc.dram_tensor("ffn_b", [C], F32, kind="ExternalInput").ap()
    rt_w = nc.dram_tensor("rt_w", [C, NEXP], F32, kind="ExternalInput").ap()
    rt_b = nc.dram_tensor("rt_b", [NEXP], F32, kind="ExternalInput").ap()
    ident_bf = nc.dram_tensor("ident_bf", [128, 128], BF16, kind="ExternalInput").ap()
    triu_bf = nc.dram_tensor("triu_bf", [128, 128], BF16, kind="ExternalInput").ap()
    ones_bf = nc.dram_tensor("ones_bf", [128, 128], BF16, kind="ExternalInput").ap()
    onesf_in = nc.dram_tensor("onesf", [1, 128], F32, kind="ExternalInput").ap()
    identf_in = nc.dram_tensor("identf", [128, 128], F32, kind="ExternalInput").ap()

    scratch_kind = "ExternalOutput" if debug else "Internal"
    xn_hbm = nc.dram_tensor("xn_hbm", [T, C], BF16, kind=scratch_kind).ap()
    xq_hbm = nc.dram_tensor("xq_hbm", [T, C], BF16, kind=scratch_kind).ap()
    wqbf_hbm = nc.dram_tensor("wqbf_hbm", [C, C], BF16, kind=scratch_kind).ap()
    g0_hbm = nc.dram_tensor("g0_hbm", [T, C], BF16, kind=scratch_kind).ap()
    if debug:
        dbg_kt = nc.dram_tensor("dbg_kt", [128, T], BF16, kind="ExternalOutput").ap()
        dbg_f0 = nc.dram_tensor("dbg_f0", [128, C], F32, kind="ExternalOutput").ap()
        dbg_f1 = nc.dram_tensor("dbg_f1", [128, C], BF16, kind="ExternalOutput").ap()
        dbg_xs = nc.dram_tensor("dbg_xs", [128, TT], F32, kind="ExternalOutput").ap()
        dbg_yt = nc.dram_tensor("dbg_yt", [128, 16 * T], BF16, kind="ExternalOutput").ap()
        dbg_negc = nc.dram_tensor("dbg_negc", [128, NH0], F32, kind="ExternalOutput").ap()

    x1_out = nc.dram_tensor("x1_out", [T, C], F32, kind="ExternalOutput").ap()
    xn1_out = nc.dram_tensor("xn1_out", [T, C], BF16, kind="ExternalOutput").ap()
    st_out = nc.dram_tensor("st_out", [NEXP, T], F32, kind="ExternalOutput").ap()

    E = ctx.enter_context
    sb = {}
    big0 = E(nc.sbuf_tensor("big0", [128, 16 * T], BF16))
    big1 = E(nc.sbuf_tensor("big1", [128, 16 * T], BF16))
    for nm, shape, dt in [
        ("xtile", [128, C], F32), ("work1", [128, C], F32), ("work2", [128, C], F32),
        ("work3", [128, C], F32),
        ("bftile0", [128, C], BF16), ("bftile1", [128, C], BF16),
        ("i8tile", [128, C], I8),
        ("k_T", [128, T], BF16), ("k_tm", [128, T], BF16),
        ("wd_h", [128, C], BF16), ("wlat", [128, C], BF16),
        ("expb0", [128, 512], BF16), ("expb1", [128, 128], BF16),
        ("rrep", [128, 512], F32), ("lntmp", [128, 512], F32),
        ("frep0", [128, C], F32), ("frep1", [128, C], BF16),
        ("onesf", [1, 128], F32), ("identf", [128, 128], F32),
        ("ident", [128, 128], BF16), ("triu", [128, 128], BF16),
        ("ones128", [128, 128], BF16),
        ("xs_all", [128, TT], F32), ("bd_sb", [128, KC], F32),
        ("blat_sb", [128, 1], F32),
        ("rtw_sb", [128, KC * NEXP], BF16), ("rtb_sb", [NEXP, 1], F32),
        ("s_tm", [128, TT * NEXP], F32), ("sttile", [NEXP, 128], F32),
        ("persist", [1, 8], F32),
        ("ln", [128, 12], F32), ("l1", [128, 12], F32),
        ("sm0", [128, 16], F32), ("sm1", [128, 16], F32),
        ("negc", [128, 1], F32), ("kam", [128, 8], F32), ("qam", [128, 8], F32),
        ("wsc", [128, 16], F32),
    ]:
        sb[nm] = E(nc.sbuf_tensor("sb_" + nm, shape, dt))
    sb["srow"] = sb["lntmp"]

    pA = E(nc.psum_tensor("pA", [128, 2048], F32))
    pB = E(nc.psum_tensor("pB", [128, 1024], F32))
    pC = E(nc.psum_tensor("pC", [128, 512], F32))
    pD = E(nc.psum_tensor("pD", [128, 1024], BF16))

    p.dma("sync", sb["ident"][:], ident_bf)
    p.dma("sync", sb["identf"][:], identf_in)
    p.dma("sync", sb["triu"][:], triu_bf)
    p.dma("sync", sb["ones128"][:], ones_bf)
    p.dma("sync", sb["onesf"][:], onesf_in)
    p.dma("sync", sb["bd_sb"][:], b_d.rearrange("(h p) -> p h", p=128), allow_slow_non_contiguous=True)
    p.dma("sync", sb["blat_sb"][:], b_lat.rearrange("(h p) -> p h", p=128), allow_slow_non_contiguous=True)
    p.dma("sync", sb["rtb_sb"][:], rt_b.rearrange("(h p) -> p h", p=NEXP), allow_slow_non_contiguous=True)
    p.dma("gpsimd", sb["wlat"][:, :].rearrange("p (c l) -> p c l", c=KC),
          w_lat.rearrange("(c p) l -> p c l", p=128))
    p.dma("gpsimd", sb["rtw_sb"][:, :].rearrange("p (c l) -> p c l", c=KC),
          rt_w.rearrange("(c p) l -> p c l", p=128))
    p.dma("sync", sb["work1"][0:1, 0:C], ffn_b.rearrange("(o n) -> o n", o=1))
    p.barrier()
    emit_bcast_row(p, sb, pB, sb["work1"][:, 0:C], sb["frep1"][:, :], C)
    p.barrier()

    for i in range(TT):
        ld = p.dma("sync", sb["xtile"][:], ffn_w[ts(i), :])
        p.wait("vector", "sync", ld)
        p.wait("scalar", "sync", ld)
        p.tick("vector", lambda e: e.tensor_reduce(
            sb["ln"][:, 0:1], sb["xtile"][:], AX.X, AluOpType.max,
            apply_absolute_value=True))
        p.tick("vector", lambda e, i=i: e.tensor_scalar(
            sb["wsc"][:, i:i + 1], sb["ln"][:, 0:1], 1e-8, 1.0 / 127.0,
            AluOpType.max, AluOpType.mult))
        v5 = p.tick("vector", lambda e, i=i: e.reciprocal(
            sb["ln"][:, 1:2], sb["wsc"][:, i:i + 1]))
        v6 = emit_quant_round(p, sb, sb["xtile"][:], sb["ln"][:, 1:2],
                              sb["bftile0"][:])
        p.wait("sync", "vector", v6)
        p.dma("sync", wqbf_hbm[ts(i), :], sb["bftile0"][:])
        p.barrier()
    for j in range(16):
        emit_col_to_rep(p, sb, pB, sb["wsc"][:, j:j + 1], sb["frep0"][:, ts(j)],
                        sb["srow"])
    p.barrier()

    for i in range(TT):
        ld = p.dma("sync", sb["xtile"][:], x_in[ts(i), :])
        p.wait("vector", "sync", ld)
        p.wait("scalar", "sync", ld)
        r_ap, mr_ap, vln = emit_ln_stats(p, sb, sb["xtile"][:], "ln")
        p.wait("scalar", "vector", vln)
        s1 = p.tick("scalar", lambda e, r_ap=r_ap, mr_ap=mr_ap: e.activation(
            sb["work1"][:], sb["xtile"][:], AF.Identity, bias=mr_ap, scale=r_ap))
        p.wait("vector", "scalar", s1)
        p.tick("vector", lambda e: e.tensor_reduce(
            sb["ln"][:, 0:1], sb["work1"][:], AX.X, AluOpType.max,
            apply_absolute_value=True))
        p.tick("vector", lambda e, i=i: e.tensor_scalar(
            sb["xs_all"][:, i:i + 1], sb["ln"][:, 0:1], 1.0 / 127.0, 1e-12,
            AluOpType.mult, AluOpType.max))
        v3 = p.tick("vector", lambda e, i=i: e.reciprocal(
            sb["ln"][:, 1:2], sb["xs_all"][:, i:i + 1]))
        v5 = p.tick("vector", lambda e: e.tensor_copy(sb["bftile1"][:], sb["work1"][:]))
        v4 = emit_quant_round(p, sb, sb["work1"][:], sb["ln"][:, 1:2],
                              sb["bftile0"][:])
        p.wait("sync", "vector", v4)
        p.dma("sync", xq_hbm[ts(i), :], sb["bftile0"][:])
        p.dma("sync", xn_hbm[ts(i), :], sb["bftile1"][:])
        p.barrier()

    for c in range(KC):
        p.dmat(big1[:, ts(c, T)], wqbf_hbm[:, ts(c)])
    for half in range(2):
        for c in range(KC):
            p.dmat(big0[:, ts(half * 16 + c, 1024)],
                   xq_hbm[ts(half, 1024), ts(c)])
    p.barrier()
    for i in range(TT):
        half, toff = i // 8, (i % 8) * 128
        for c in range(KC):
            stat = big0[:, (half * 16 + c) * 1024 + toff:(half * 16 + c) * 1024 + toff + 128]
            for nb in range(4):
                mov = big1[:, c * T + nb * 512:c * T + (nb + 1) * 512]
                tmm = p.tick("tensor", lambda e, stat=stat, mov=mov, nb=nb, c=c: e.matmul(
                    pA[:, ts(nb, 512)], stat, mov, start=(c == 0), stop=(c == KC - 1)))
        p.wait("vector", "tensor", tmm)
        p.tick("vector", lambda e, i=i: e.scalar_tensor_tensor(
            sb["work1"][:], pA[:, 0:2048], sb["xs_all"][:, i:i + 1], sb["frep0"][:, :],
            AluOpType.mult, AluOpType.mult))
        v2 = p.tick("vector", lambda e: e.tensor_tensor(
            sb["work2"][:], sb["work1"][:], sb["frep1"][:, :], AluOpType.add))
        p.wait("scalar", "vector", v2)
        s1 = p.tick("scalar", lambda e: e.activation(
            sb["bftile0"][:], sb["work2"][:], AF.Silu))
        p.wait("sync", "scalar", s1)
        p.dma("sync", g0_hbm[ts(i), :], sb["bftile0"][:])
        p.barrier()

    for c in range(KC):
        p.dmat(big0[:, ts(c, T)], xn_hbm[:, ts(c)])
    p.barrier()
    for c in range(KC):
        stat = sb["wlat"][:, ts(c)]
        for nb in range(4):
            mov = big0[:, c * T + nb * 512:c * T + (nb + 1) * 512]
            tmm = p.tick("tensor", lambda e, stat=stat, mov=mov, nb=nb, c=c: e.matmul(
                pA[:, ts(nb, 512)], stat, mov, start=(c == 0), stop=(c == KC - 1)))
    p.wait("vector", "tensor", tmm)
    for nb in range(4):
        p.tick("vector", lambda e, nb=nb: e.tensor_scalar(
            sb["k_T"][:, ts(nb, 512)], pA[:, ts(nb, 512)], sb["blat_sb"][:, 0:1],
            None, AluOpType.add))
    p.barrier()
    for j in range(TT):
        tm = p.tick("tensor", lambda e, j=j: e.transpose(
            pD[:, 0:128], sb["k_T"][:, ts(j)], sb["ident"][:]))
        p.wait("vector", "tensor", tm)
        tv = p.tick("vector", lambda e, j=j: e.tensor_copy(sb["k_tm"][:, ts(j)], pD[:, 0:128]))
        p.wait("tensor", "vector", tv)
    p.barrier()
    for j in range(TT):
        p.tick("vector", lambda e, j=j: e.scalar_tensor_tensor(
            sb["expb1"][:], sb["k_tm"][:, ts(j)], 1.0, sb["k_tm"][:, ts(j)],
            AluOpType.mult, AluOpType.mult, accum_out=sb["sm1"][:, j:j + 1]))
    p.tick("vector", lambda e: e.tensor_reduce(
        sb["kam"][:, 5:6], sb["sm1"][:, 0:TT], AX.X, AluOpType.max))
    emit_col_max11(p, sb, pB, sb["kam"][:, 5:6], 128)
    p.tick("vector", lambda e: e.tensor_copy(
        sb["persist"][0:1, 0:1], sb["srow"][0:1, 128:129]))
    p.barrier()

    if debug:
        p.store(dbg_kt, sb["k_T"][:])
        p.store(dbg_f0, sb["frep0"][:])
        p.store(dbg_f1, sb["frep1"][:])
        p.store(dbg_xs, sb["xs_all"][:])
        p.barrier()
    for h in range(NH0):
        ldw = p.dma("gpsimd", sb["wd_h"][:, :].rearrange("p (c l) -> p c l", c=KC),
                    w_d.rearrange("(c p) l -> p c l", p=128)[:, :, ts(h)])
        p.wait("tensor", "gpsimd", ldw)
        for c in range(KC):
            stat = sb["wd_h"][:, ts(c)]
            for nb in range(4):
                mov = big0[:, c * T + nb * 512:c * T + (nb + 1) * 512]
                tmm = p.tick("tensor", lambda e, stat=stat, mov=mov, nb=nb, c=c: e.matmul(
                    pA[:, ts(nb, 512)], stat, mov, start=(c == 0), stop=(c == KC - 1)))
        p.wait("vector", "tensor", tmm)
        for nb in range(4):
            p.tick("vector", lambda e, nb=nb, h=h: e.tensor_scalar(
                sb["bftile1"][:, ts(nb, 512)], pA[:, ts(nb, 512)],
                sb["bd_sb"][:, h:h + 1], None, AluOpType.add))
        vq = p.tick("vector", lambda e: e.tensor_reduce(
            sb["qam"][:, 4:5], sb["bftile1"][:], AX.X, AluOpType.max,
            apply_absolute_value=True))
        p.wait("tensor", "vector", vq)
        emit_col_sumsq11(p, sb, pB, sb["qam"][:, 4:5], 129)
        emit_negc_from_sq(p, sb, pB, sb["persist"][0:1, 0:1],
                          sb["srow"][0:1, 129:130], sb["negc"][:, 0:1])
        if debug:
            p.tick("vector", lambda e, h=h: e.tensor_copy(
                sb["sm0"][:, h:h + 1], sb["negc"][:, 0:1]))
        p.barrier()

        def yT_store(hh, Q, psum_y, rrep, h=h):
            return p.tick("vector", lambda e, Q=Q, h=h: e.tensor_tensor(
                big1[:, h * T + Q * 512:h * T + (Q + 1) * 512], psum_y, rrep,
                AluOpType.mult))

        emit_attention(
            p, sb, (pA, pB),
            kT_ap=lambda h_, j: sb["k_T"][:, ts(j)],
            ktm_ap=lambda h_, j: sb["k_tm"][:, ts(j)],
            qT_ap=lambda h_, Q: sb["bftile1"][:, ts(Q, 512)],
            yT_store=yT_store,
            negc_ap=lambda h_: sb["negc"][:, 0:1],
            nheads_note=1)
        p.barrier()

    if debug:
        p.store(dbg_yt, big1[:, :])
        p.store(dbg_negc, sb["sm0"][:, 0:NH0])
        p.barrier()
    for c in range(KC):
        p.dma("gpsimd", big0[:, ts(c, T)], w_proj[ts(c), :])
    p.dma("sync", sb["work1"][0:1, 0:C], b_proj.rearrange("(o n) -> o n", o=1))
    p.barrier()
    emit_bcast_row(p, sb, pB, sb["work1"][:, 0:C], sb["frep0"][:, :], C)
    p.barrier()
    for i in range(TT):
        for c in range(KC):
            stat = big1[:, c * T + i * 128:c * T + i * 128 + 128]
            for nb in range(4):
                mov = big0[:, c * T + nb * 512:c * T + (nb + 1) * 512]
                tmm = p.tick("tensor", lambda e, stat=stat, mov=mov, nb=nb, c=c: e.matmul(
                    pA[:, ts(nb, 512)], stat, mov, start=(c == 0), stop=(c == KC - 1)))
        ldg = p.dma("sync", sb["bftile0"][:], g0_hbm[ts(i), :])
        ldx = p.dma("sync", sb["xtile"][:], x_in[ts(i), :])
        p.wait("vector", "tensor", tmm)
        p.wait("vector", "sync", ldx)
        p.tick("vector", lambda e: e.tensor_tensor(
            sb["work1"][:], pA[:, 0:2048], sb["frep0"][:, :], AluOpType.add))
        p.tick("vector", lambda e: e.tensor_tensor(
            sb["work2"][:], sb["work1"][:], sb["bftile0"][:], AluOpType.mult))
        v3 = p.tick("vector", lambda e: e.tensor_tensor(
            sb["work3"][:], sb["work2"][:], sb["xtile"][:], AluOpType.add))
        p.wait("sync", "vector", v3)
        p.store(x1_out[ts(i), :], sb["work3"][:])
        p.wait("scalar", "vector", v3)
        r_ap, mr_ap, vln = emit_ln_stats(p, sb, sb["work3"][:], "l1")
        p.wait("scalar", "vector", vln)
        s1 = p.tick("scalar", lambda e, r_ap=r_ap, mr_ap=mr_ap: e.activation(
            sb["work1"][:], sb["work3"][:], AF.Identity, bias=mr_ap, scale=r_ap))
        p.wait("vector", "scalar", s1)
        v4 = p.tick("vector", lambda e: e.tensor_copy(sb["bftile1"][:], sb["work1"][:]))
        p.wait("sync", "vector", v4)
        p.store(xn1_out[ts(i), :], sb["bftile1"][:])
        p.wait("tensor", "vector", v4)
        for c in range(KC):
            tm = p.tick("tensor", lambda e, c=c: e.transpose(
                pD[:, 0:128], sb["bftile1"][:, ts(c)], sb["ident"][:]))
            p.wait("vector", "tensor", tm)
            tv = p.tick("vector", lambda e, c=c: e.tensor_copy(
                sb["bftile0"][:, ts(c)], pD[:, 0:128]))
            p.wait("tensor", "vector", tv)
        for c in range(KC):
            tmm3 = p.tick("tensor", lambda e, c=c: e.matmul(
                pC[0:NEXP, 0:128], sb["rtw_sb"][:, c * NEXP:(c + 1) * NEXP],
                sb["bftile0"][:, ts(c)], start=(c == 0), stop=(c == KC - 1)))
        p.wait("vector", "tensor", tmm3)
        v5 = p.tick("vector", lambda e: e.tensor_scalar(
            sb["lntmp"][0:NEXP, 0:128], pC[0:NEXP, 0:128], sb["rtb_sb"][0:NEXP, 0:1],
            None, AluOpType.add))
        p.wait("tensor", "vector", v5)
        tm4 = p.tick("tensor", lambda e: e.transpose(
            pC[:, 0:16], sb["lntmp"][0:NEXP, 0:128], sb["identf"][0:NEXP, 0:NEXP]))
        p.wait("vector", "tensor", tm4)
        v6 = p.tick("vector", lambda e: e.tensor_reduce(
            sb["sm0"][:, 2:3], pC[:, 0:16], AX.X, AluOpType.max))
        v7 = p.tick("vector", lambda e: e.tensor_scalar(
            sb["sm0"][:, 3:4], sb["sm0"][:, 2:3], -1.0, None, AluOpType.mult))
        p.wait("scalar", "vector", v7)
        s2 = p.ticka("scalar", lambda e: e.activation(
            sb["sm1"][:, 0:16], pC[:, 0:16], AF.Exp, bias=sb["sm0"][:, 3:4],
            accum_out=sb["sm0"][:, 2:3]))
        p.wait("vector", "scalar", s2)
        p.tick("vector", lambda e: e.reciprocal(sb["sm0"][:, 3:4], sb["sm0"][:, 2:3]))
        v9 = p.tick("vector", lambda e, i=i: e.tensor_scalar(
            sb["s_tm"][:, i * NEXP:(i + 1) * NEXP], sb["sm1"][:, 0:16],
            sb["sm0"][:, 3:4], None, AluOpType.mult))
        p.wait("tensor", "vector", v9)
        tm5 = p.tick("tensor", lambda e, i=i: e.transpose(
            pC[0:NEXP, 0:128], sb["s_tm"][:, i * NEXP:(i + 1) * NEXP], sb["identf"][:]))
        p.wait("vector", "tensor", tm5)
        v10 = p.tick("vector", lambda e, i=i: e.tensor_copy(
            sb["sttile"][:], pC[0:NEXP, 0:128]))
        p.wait("sync", "vector", v10)
        p.store(st_out[:, ts(i)], sb["sttile"][:])
        p.barrier()
    p.finalize()
    p.build_block()
    return nc


def build_program_b():
    nc = bass.Bass("TRN2", target_bir_lowering=False)
    ctx = contextlib.ExitStack()
    nc._ctx = ctx
    p = Prog(nc, ctx)

    x1_in = nc.dram_tensor("x1", [T, C], F32, kind="ExternalInput").ap()
    xn1_in = nc.dram_tensor("xn1", [T, C], BF16, kind="ExternalInput").ap()
    sall_in = nc.dram_tensor("sall", [128, T], F32, kind="ExternalInput").ap()
    sown_in = nc.dram_tensor("sown", [NEXP, T], F32, kind="ExternalInput").ap()
    wq = nc.dram_tensor("wq", [C, C], F32, kind="ExternalInput").ap()
    bq = nc.dram_tensor("bq", [C], F32, kind="ExternalInput").ap()
    wk = nc.dram_tensor("wk", [C, 512], F32, kind="ExternalInput").ap()
    bk = nc.dram_tensor("bk", [512], F32, kind="ExternalInput").ap()
    wv = nc.dram_tensor("wv", [C, 512], F32, kind="ExternalInput").ap()
    bv = nc.dram_tensor("bv", [512], F32, kind="ExternalInput").ap()
    wo = nc.dram_tensor("wo", [512, C], F32, kind="ExternalInput").ap()
    bo = nc.dram_tensor("bo", [C], F32, kind="ExternalInput").ap()
    ew = nc.dram_tensor("ew", [C, C], F32, kind="ExternalInput").ap()
    eb_in = nc.dram_tensor("eb", [C], F32, kind="ExternalInput").ap()
    cos_in = nc.dram_tensor("cos_t", [T, 64], F32, kind="ExternalInput").ap()
    sin_in = nc.dram_tensor("sin_t", [T, 64], F32, kind="ExternalInput").ap()
    ident_bf = nc.dram_tensor("ident_bf", [128, 128], BF16, kind="ExternalInput").ap()
    triu_bf = nc.dram_tensor("triu_bf", [128, 128], BF16, kind="ExternalInput").ap()
    ones_bf = nc.dram_tensor("ones_bf", [128, 128], BF16, kind="ExternalInput").ap()
    onesf_in = nc.dram_tensor("onesf", [1, 128], F32, kind="ExternalInput").ap()
    identf_in = nc.dram_tensor("identf", [128, 128], F32, kind="ExternalInput").ap()
    sel16_in = nc.dram_tensor("sel16", [128, NEXP], F32, kind="ExternalInput").ap()
    sel16t_in = nc.dram_tensor("sel16t", [NEXP, 128], F32, kind="ExternalInput").ap()

    qro_hbm = nc.dram_tensor("qro_hbm", [T, 512], BF16, kind="Internal").ap()
    kro_hbm = nc.dram_tensor("kro_hbm", [T, 512], BF16, kind="Internal").ap()

    out_hbm = nc.dram_tensor("out", [T, C], F32, kind="ExternalOutput").ap()

    E = ctx.enter_context
    sb = {}
    big0 = E(nc.sbuf_tensor("big0", [128, 16 * T], BF16))
    att64 = E(nc.sbuf_tensor("att64", [128, 16 * T], BF16))
    S0, S1, S2, S3 = 0, 8192, 16384, 24576
    for nm, shape, dt in [
        ("xtile", [128, C], F32), ("work1", [128, C], F32), ("work2", [128, C], F32),
        ("work3", [128, C], F32),
        ("bftile0", [128, C], BF16), ("bftile1", [128, C], BF16),
        ("s_sb", [128, T], F32),
        ("sown_tm", [128, TT * NEXP], F32), ("m_all", [128, TT], F32),
        ("cos_sb", [128, TT * 64], F32), ("sin_sb", [128, TT * 64], F32),
        ("qtile", [128, 512], F32), ("robf", [128, 512], BF16),
        ("expb0", [128, 512], BF16), ("expb1", [128, 128], BF16),
        ("rrep", [128, 512], F32), ("lntmp", [128, 512], F32),
        ("brep", [128, 512], F32),
        ("ebrep", [128, C], BF16), ("borep", [128, C], BF16),
        ("onesf", [1, 128], F32), ("identf", [128, 128], F32),
        ("ident", [128, 128], BF16), ("triu", [128, 128], BF16),
        ("ones128", [128, 128], BF16),
        ("sel16", [128, NEXP], F32), ("sel16t", [NEXP, 128], F32),
        ("lo", [128, 1], F32), ("hi", [128, 1], F32), ("tau", [128, 1], F32),
        ("cntr", [128, 1], F32), ("geb", [128, 1], F32), ("bis", [128, 8], F32),
        ("taurep", [128, NEXP], F32),
        ("negc", [128, 4], F32), ("kam", [128, 8], F32), ("qam", [128, 8], F32),
        ("nm0", [128, 8], F32), ("nm1", [128, 8], F32),
        ("sm0", [128, 16], F32), ("sm1", [128, 16], F32),
    ]:
        sb[nm] = E(nc.sbuf_tensor("sb_" + nm, shape, dt))
    sb["srow"] = sb["lntmp"]

    pA = E(nc.psum_tensor("pA", [128, 2048], F32))
    pB = E(nc.psum_tensor("pB", [128, 1024], F32))
    pC = E(nc.psum_tensor("pC", [128, 512], F32))
    pD = E(nc.psum_tensor("pD", [128, 1024], BF16))

    p.dma("sync", sb["ident"][:], ident_bf)
    p.dma("sync", sb["identf"][:], identf_in)
    p.dma("sync", sb["triu"][:], triu_bf)
    p.dma("sync", sb["ones128"][:], ones_bf)
    p.dma("sync", sb["onesf"][:], onesf_in)
    p.dma("sync", sb["sel16"][:], sel16_in)
    p.dma("sync", sb["sel16t"][:], sel16t_in)
    p.dma("sync", sb["s_sb"][:], sall_in)
    p.dma("sync", sb["cos_sb"][:, :].rearrange("p (t l) -> p t l", t=TT),
          cos_in.rearrange("(t p) l -> p t l", p=128))
    p.dma("sync", sb["sin_sb"][:, :].rearrange("p (t l) -> p t l", t=TT),
          sin_in.rearrange("(t p) l -> p t l", p=128))
    p.barrier()

    p.tick("vector", lambda e: e.memset(sb["lo"][:], 0.0))
    p.tick("vector", lambda e: e.memset(sb["hi"][:], 1.0))
    for it in range(BIS_ITERS):
        p.tick("vector", lambda e: e.tensor_tensor(
            sb["bis"][:, 4:5], sb["lo"][:], sb["hi"][:], AluOpType.add))
        p.tick("vector", lambda e: e.tensor_scalar(
            sb["tau"][:], sb["bis"][:, 4:5], 0.5, None, AluOpType.mult))
        v2 = p.ticka("vector", lambda e: e.tensor_scalar(
            sb["work1"][:, 0:T], sb["s_sb"][:], sb["tau"][:, 0:1], 0.0,
            AluOpType.is_ge, AluOpType.add, accum_out=sb["bis"][:, 0:1]))
        p.wait("tensor", "vector", v2)
        tm1 = p.tick("tensor", lambda e: e.matmul(
            pC[0:NEXP, 0:1], sb["sel16"][:], sb["bis"][:, 0:1], start=True, stop=True))
        p.wait("vector", "tensor", tm1)
        v3 = p.tick("vector", lambda e: e.tensor_copy(sb["sm0"][0:NEXP, 0:1], pC[0:NEXP, 0:1]))
        p.wait("tensor", "vector", v3)
        tm2 = p.tick("tensor", lambda e: e.matmul(
            pC[:, 4:5], sb["sel16t"][:], sb["sm0"][0:NEXP, 0:1], start=True, stop=True))
        p.wait("vector", "tensor", tm2)
        p.tick("vector", lambda e: e.tensor_copy(sb["cntr"][:], pC[:, 4:5]))
        p.tick("vector", lambda e: e.tensor_scalar(
            sb["geb"][:], sb["cntr"][:], float(KSEL), None, AluOpType.is_ge))
        p.tick("vector", lambda e: e.tensor_tensor(
            sb["bis"][:, 1:2], sb["tau"][:], sb["lo"][:], AluOpType.subtract))
        p.tick("vector", lambda e: e.scalar_tensor_tensor(
            sb["bis"][:, 5:6], sb["bis"][:, 1:2], sb["geb"][:, 0:1], sb["lo"][:],
            AluOpType.mult, AluOpType.add))
        p.tick("vector", lambda e: e.tensor_copy(sb["lo"][:], sb["bis"][:, 5:6]))
        p.tick("vector", lambda e: e.tensor_scalar(
            sb["bis"][:, 2:3], sb["geb"][:], -1.0, 1.0, AluOpType.mult, AluOpType.add))
        p.tick("vector", lambda e: e.tensor_tensor(
            sb["bis"][:, 3:4], sb["tau"][:], sb["hi"][:], AluOpType.subtract))
        p.tick("vector", lambda e: e.scalar_tensor_tensor(
            sb["bis"][:, 6:7], sb["bis"][:, 3:4], sb["bis"][:, 2:3], sb["hi"][:],
            AluOpType.mult, AluOpType.add))
        p.tick("vector", lambda e: e.tensor_copy(sb["hi"][:], sb["bis"][:, 6:7]))
    p.barrier()
    tm = p.tick("tensor", lambda e: e.transpose(
        pC[0:1, 0:NEXP], sb["lo"][0:NEXP, 0:1], sb["identf"][0:NEXP, 0:NEXP]))
    p.wait("vector", "tensor", tm)
    tv = p.tick("vector", lambda e: e.tensor_copy(sb["srow"][0:1, 0:NEXP], pC[0:1, 0:NEXP]))
    p.wait("tensor", "vector", tv)
    tm2 = p.tick("tensor", lambda e: e.matmul(
        pC[:, 32:32 + NEXP], sb["onesf"][0:1, :], sb["srow"][0:1, 0:NEXP],
        start=True, stop=True))
    p.wait("vector", "tensor", tm2)
    p.tick("vector", lambda e: e.tensor_copy(sb["taurep"][:], pC[:, 32:32 + NEXP]))
    p.barrier()
    ld = p.dma("sync", sb["work1"][0:NEXP, 0:T], sown_in)
    p.barrier()
    for i in range(TT):
        tm = p.tick("tensor", lambda e, i=i: e.transpose(
            pC[:, 0:NEXP], sb["work1"][0:NEXP, ts(i)], sb["identf"][0:NEXP, 0:NEXP]))
        p.wait("vector", "tensor", tm)
        v1 = p.tick("vector", lambda e, i=i: e.tensor_copy(
            sb["sown_tm"][:, i * NEXP:(i + 1) * NEXP], pC[:, 0:NEXP]))
        p.wait("tensor", "vector", v1)
        p.tick("vector", lambda e, i=i: e.tensor_tensor(
            sb["sm0"][:, 0:NEXP], sb["sown_tm"][:, i * NEXP:(i + 1) * NEXP],
            sb["taurep"][:], AluOpType.is_ge))
        p.ticka("vector", lambda e, i=i: e.scalar_tensor_tensor(
            sb["sm1"][:, 0:NEXP], sb["sown_tm"][:, i * NEXP:(i + 1) * NEXP], 1.0,
            sb["sm0"][:, 0:NEXP], AluOpType.mult, AluOpType.mult,
            accum_out=sb["m_all"][:, i:i + 1]))
    p.barrier()

    for c in range(KC):
        p.dmat(big0[:, ts(c, T)], xn1_in[:, ts(c)])
    p.barrier()

    wsh = lambda c, w=512: att64[:, S3 + c * w:S3 + (c + 1) * w]
    for c in range(KC):
        ldc = p.dma("gpsimd", sb["bftile0"][:], wq[ts(c), :])
        p.wait("vector", "gpsimd", ldc)
        for h in range(NKV):
            p.tick("vector", lambda e, h=h: e.tensor_tensor(
                sb["robf"][:, 0:128],
                sb["bftile0"][:, (4 * h) * 128:(4 * h + 1) * 128],
                sb["bftile0"][:, (4 * h + 1) * 128:(4 * h + 2) * 128], AluOpType.add))
            p.tick("vector", lambda e, h=h: e.tensor_tensor(
                sb["robf"][:, 128:256],
                sb["robf"][:, 0:128],
                sb["bftile0"][:, (4 * h + 2) * 128:(4 * h + 3) * 128], AluOpType.add))
            vw = p.tick("vector", lambda e, c=c, h=h: e.tensor_tensor(
                att64[:, S3 + c * 512 + h * 128:S3 + c * 512 + (h + 1) * 128],
                sb["robf"][:, 128:256],
                sb["bftile0"][:, (4 * h + 3) * 128:(4 * h + 4) * 128], AluOpType.add))
        p.wait("gpsimd", "vector", vw)
    p.dma("sync", sb["work1"][0:1, 0:C], bq.rearrange("(o n) -> o n", o=1))
    p.barrier()
    for h in range(NKV):
        p.tick("vector", lambda e, h=h: e.tensor_tensor(
            sb["work2"][0:1, h * 256:h * 256 + 128],
            sb["work1"][0:1, (4 * h) * 128:(4 * h + 1) * 128],
            sb["work1"][0:1, (4 * h + 1) * 128:(4 * h + 2) * 128], AluOpType.add))
        p.tick("vector", lambda e, h=h: e.tensor_tensor(
            sb["work2"][0:1, h * 256 + 128:h * 256 + 256],
            sb["work1"][0:1, (4 * h + 2) * 128:(4 * h + 3) * 128],
            sb["work1"][0:1, (4 * h + 3) * 128:(4 * h + 4) * 128], AluOpType.add))
    for h in range(NKV):
        p.tick("vector", lambda e, h=h: e.tensor_tensor(
            sb["work3"][0:1, h * 128:(h + 1) * 128],
            sb["work2"][0:1, h * 256:h * 256 + 128],
            sb["work2"][0:1, h * 256 + 128:h * 256 + 256], AluOpType.add))
    p.barrier()
    emit_bcast_row(p, sb, pB, sb["work3"][:, 0:C], sb["brep"][:, 0:512], 512)
    p.barrier()

    def qkv_proj(brep_ap, dst_hbm, do_rope, norm_off=None):
        for i in range(TT):
            for c in range(KC):
                stat = big0[:, c * T + i * 128:c * T + i * 128 + 128]
                mov = wsh(c)
                tmm = p.tick("tensor", lambda e, stat=stat, mov=mov, c=c: e.matmul(
                    pC[:, 0:512], stat, mov, start=(c == 0), stop=(c == KC - 1)))
            p.wait("vector", "tensor", tmm)
            v1 = p.tick("vector", lambda e: e.tensor_tensor(
                sb["qtile"][:, 0:512], pC[:, 0:512], brep_ap, AluOpType.add))
            p.wait("tensor", "vector", v1)
            if norm_off is not None:
                dst = sb["nm0"] if i % 2 == 0 else sb["nm1"]
                srcb = sb["nm1"] if i % 2 == 0 else sb["nm0"]
                for h in range(NKV):
                    p.tick("vector", lambda e, h=h: e.scalar_tensor_tensor(
                        sb["expb1"][:], sb["qtile"][:, h * 128:(h + 1) * 128], 1.0,
                        sb["qtile"][:, h * 128:(h + 1) * 128],
                        AluOpType.mult, AluOpType.mult, accum_out=sb["qam"][:, 0:1]))
                    if i == 0:
                        p.tick("vector", lambda e, h=h, dst=dst, norm_off=norm_off: e.tensor_copy(
                            dst[:, norm_off + h:norm_off + h + 1], sb["qam"][:, 0:1]))
                    else:
                        p.tick("vector", lambda e, h=h, dst=dst, srcb=srcb, norm_off=norm_off: e.tensor_tensor(
                            dst[:, norm_off + h:norm_off + h + 1],
                            srcb[:, norm_off + h:norm_off + h + 1],
                            sb["qam"][:, 0:1], AluOpType.max))
            if do_rope:
                for h in range(NKV):
                    cs = sb["cos_sb"][:, i * 64:(i + 1) * 64]
                    sn = sb["sin_sb"][:, i * 64:(i + 1) * 64]
                    x1e = sb["qtile"][:, h * 128:(h + 1) * 128:2]
                    x2e = sb["qtile"][:, h * 128 + 1:(h + 1) * 128:2]
                    p.tick("vector", lambda e, x1e=x1e, cs=cs: e.tensor_tensor(
                        sb["work2"][:, 0:64], x1e, cs, AluOpType.mult))
                    p.tick("vector", lambda e, x2e=x2e, sn=sn: e.tensor_tensor(
                        sb["work2"][:, 64:128], x2e, sn, AluOpType.mult))
                    p.tick("vector", lambda e, h=h: e.tensor_tensor(
                        sb["robf"][:, h * 128:h * 128 + 64], sb["work2"][:, 0:64],
                        sb["work2"][:, 64:128], AluOpType.subtract))
                    p.tick("vector", lambda e, x1e=x1e, sn=sn: e.tensor_tensor(
                        sb["work2"][:, 0:64], x1e, sn, AluOpType.mult))
                    p.tick("vector", lambda e, x2e=x2e, cs=cs: e.tensor_tensor(
                        sb["work2"][:, 64:128], x2e, cs, AluOpType.mult))
                    vr = p.tick("vector", lambda e, h=h: e.tensor_tensor(
                        sb["robf"][:, h * 128 + 64:(h + 1) * 128], sb["work2"][:, 0:64],
                        sb["work2"][:, 64:128], AluOpType.add))
                p.wait("sync", "vector", vr)
                p.dma("sync", dst_hbm[ts(i), :], sb["robf"][:, 0:512])
            else:
                p.tick("vector", lambda e, i=i: e.tensor_copy(
                    att64[:, S2 + i * 512:S2 + (i + 1) * 512], sb["qtile"][:, 0:512]))
            p.barrier()

    qkv_proj(sb["brep"][:, 0:512], qro_hbm, True, norm_off=0)
    p.dma("gpsimd", att64[:, S3:S3 + KC * 512].rearrange("p (c l) -> p c l", c=KC),
          wk.rearrange("(c p) l -> p c l", p=128))
    p.dma("sync", sb["work1"][0:1, 0:512], bk.rearrange("(o n) -> o n", o=1))
    p.barrier()
    emit_bcast_row(p, sb, pB, sb["work1"][:, 0:C], sb["brep"][:, 0:512], 512)
    p.barrier()
    qkv_proj(sb["brep"][:, 0:512], kro_hbm, True, norm_off=4)
    p.dma("gpsimd", att64[:, S3:S3 + KC * 512].rearrange("p (c l) -> p c l", c=KC),
          wv.rearrange("(c p) l -> p c l", p=128))
    p.dma("sync", sb["work1"][0:1, 0:512], bv.rearrange("(o n) -> o n", o=1))
    p.barrier()
    emit_bcast_row(p, sb, pB, sb["work1"][:, 0:C], sb["brep"][:, 0:512], 512)
    p.barrier()
    qkv_proj(sb["brep"][:, 0:512], None, False)
    p.barrier()
    for h in range(NKV):
        p.dmat(att64[:, S0 + h * T:S0 + (h + 1) * T], qro_hbm[:, ts(h)])
        p.dmat(att64[:, S1 + h * T:S1 + (h + 1) * T], kro_hbm[:, ts(h)])
    p.barrier()
    for h in range(NKV):
        emit_col_max11(p, sb, pB, sb["nm1"][:, h:h + 1], 129)
        emit_col_max11(p, sb, pB, sb["nm1"][:, 4 + h:4 + h + 1], 128)
        emit_negc_from_sq(p, sb, pB, sb["srow"][0:1, 128:129],
                          sb["srow"][0:1, 129:130], sb["negc"][:, h:h + 1])
        p.barrier()

    def yT_store(h, Q, psum_y, rrep):
        return p.tick("vector", lambda e, Q=Q, h=h: e.tensor_tensor(
            att64[:, S3 + h * T + Q * 512:S3 + h * T + (Q + 1) * 512], psum_y, rrep,
            AluOpType.mult))

    emit_attention(
        p, sb, (pA, pB),
        kT_ap=lambda h, j: att64[:, S1 + h * T + j * 128:S1 + h * T + (j + 1) * 128],
        ktm_ap=lambda h, j: att64[:, S2 + j * 512 + h * 128:S2 + j * 512 + (h + 1) * 128],
        qT_ap=lambda h, Q: att64[:, S0 + h * T + Q * 512:S0 + h * T + (Q + 1) * 512],
        yT_store=yT_store,
        negc_ap=lambda h: sb["negc"][:, h:h + 1],
        nheads_note=NKV)
    p.barrier()

    for c in range(KC):
        p.dma("gpsimd", big0[:, ts(c, T)], ew[ts(c), :])
    for c in range(NKV):
        p.dma("gpsimd", att64[:, S0 + c * T:S0 + (c + 1) * T], wo[ts(c), :])
    p.dma("sync", sb["work1"][0:1, 0:C], eb_in.rearrange("(o n) -> o n", o=1))
    p.barrier()
    emit_bcast_row(p, sb, pB, sb["work1"][:, 0:C], sb["ebrep"][:, :], C)
    p.barrier()
    p.dma("sync", sb["work1"][0:1, 0:C], bo.rearrange("(o n) -> o n", o=1))
    p.barrier()
    emit_bcast_row(p, sb, pB, sb["work1"][:, 0:C], sb["borep"][:, :], C)
    p.barrier()
    for half in range(2):
        for c in range(KC):
            p.dmat(att64[:, S1 + c * 1024:S1 + (c + 1) * 1024],
                   xn1_in[ts(half, 1024), ts(c)])
        p.barrier()
        for il in range(8):
            i = half * 8 + il
            for c in range(KC):
                stat = att64[:, S1 + c * 1024 + il * 128:S1 + c * 1024 + il * 128 + 128]
                for nb in range(4):
                    mov = big0[:, c * T + nb * 512:c * T + (nb + 1) * 512]
                    tme = p.tick("tensor", lambda e, stat=stat, mov=mov, nb=nb, c=c: e.matmul(
                        pA[:, ts(nb, 512)], stat, mov, start=(c == 0), stop=(c == KC - 1)))
            p.wait("vector", "tensor", tme)
            v1 = p.tick("vector", lambda e: e.tensor_tensor(
                sb["work1"][:], pA[:, 0:2048], sb["ebrep"][:, :], AluOpType.add))
            p.wait("scalar", "vector", v1)
            s1 = p.tick("scalar", lambda e, i=i: e.activation(
                sb["bftile0"][:], sb["work1"][:], AF.Silu, scale=sb["m_all"][:, i:i + 1]))
            p.wait("tensor", "vector", v1)
            for c in range(NKV):
                stat = att64[:, S3 + c * T + i * 128:S3 + c * T + i * 128 + 128]
                for nb in range(4):
                    mov = att64[:, S0 + c * T + nb * 512:S0 + c * T + (nb + 1) * 512]
                    tmo = p.tick("tensor", lambda e, stat=stat, mov=mov, nb=nb, c=c: e.matmul(
                        pA[:, ts(nb, 512)], stat, mov, start=(c == 0), stop=(c == NKV - 1)))
            ldx = p.dma("sync", sb["xtile"][:], x1_in[ts(i), :])
            p.wait("vector", "tensor", tmo)
            p.wait("vector", "scalar", s1)
            p.wait("vector", "sync", ldx)
            p.tick("vector", lambda e: e.tensor_tensor(
                sb["work2"][:], pA[:, 0:2048], sb["borep"][:, :], AluOpType.add))
            p.tick("vector", lambda e: e.tensor_tensor(
                sb["work3"][:], sb["work2"][:], sb["bftile0"][:], AluOpType.mult))
            v4 = p.tick("vector", lambda e: e.tensor_tensor(
                sb["work2"][:], sb["work3"][:], sb["xtile"][:], AluOpType.add))
            p.wait("sync", "vector", v4)
            p.store(out_hbm[ts(i), :], sb["work2"][:])
            p.barrier()
    p.finalize()
    p.build_block()
    return nc


def make_consts():
    return dict(
        ident_bf=np.eye(128, dtype=BF),
        identf=np.eye(128, dtype=np.float32),
        triu_bf=np.triu(np.ones((128, 128), np.float32)).astype(BF),
        ones_bf=np.ones((128, 128), dtype=BF),
        onesf=np.ones((1, 128), np.float32),
    )


def make_rope_tables():
    inv = 1.0 / 10000.0 ** (np.arange(64, dtype=np.float64) * 2.0 / 128.0)
    emb = np.arange(T, dtype=np.float64)[:, None] * inv[None, :]
    return np.cos(emb).astype(np.float32), np.sin(emb).astype(np.float32)


def make_sel16():
    sel = np.zeros((128, NEXP), np.float32)
    for p_ in range(128):
        sel[p_, p_ % NEXP] = 1.0
    return sel, np.ascontiguousarray(sel.T)


_CACHE = {}


def kernel(**inputs):
    xs = np.ascontiguousarray(np.asarray(inputs["L0_attn_x"], np.float32))
    consts = make_consts()
    cos_t, sin_t = make_rope_tables()
    sel16, sel16t = make_sel16()

    if "A" not in _CACHE:
        _CACHE["A"] = build_program_a()
    ncA = _CACHE["A"]
    wcom = dict(
        w_lat=np.ascontiguousarray(inputs["L0_attn_W_lat"], np.float32),
        b_lat=np.ascontiguousarray(inputs["L0_attn_b_lat"], np.float32),
        w_d=np.ascontiguousarray(inputs["L0_attn_W_d"], np.float32),
        b_d=np.ascontiguousarray(inputs["L0_attn_b_d"], np.float32),
        w_proj=np.ascontiguousarray(inputs["L0_attn_W_proj"], np.float32),
        b_proj=np.ascontiguousarray(inputs["L0_attn_b_proj"], np.float32),
        ffn_w=np.ascontiguousarray(inputs["L0_ffn_weight"], np.float32),
        ffn_b=np.ascontiguousarray(inputs["L0_ffn_bias"], np.float32),
        rt_w=np.ascontiguousarray(inputs["L1_ffn_router_w"], np.float32),
        rt_b=np.ascontiguousarray(inputs["L1_ffn_router_b"], np.float32),
        **consts,
    )
    in_maps = [dict(x=np.ascontiguousarray(xs[b]), **wcom) for b in range(NCORES)]
    resA = run_bass_kernel_spmd(ncA, in_maps, core_ids=list(range(NCORES)))
    x1 = [resA.results[b]["x1_out"] for b in range(NCORES)]
    xn1 = [resA.results[b]["xn1_out"] for b in range(NCORES)]
    s_T = [resA.results[b]["st_out"] for b in range(NCORES)]
    sall = np.ascontiguousarray(np.concatenate(s_T, axis=0))

    if "B" not in _CACHE:
        _CACHE["B"] = build_program_b()
    ncB = _CACHE["B"]
    wcomB = dict(
        wq=np.ascontiguousarray(inputs["L1_attn_Wq"], np.float32),
        bq=np.ascontiguousarray(inputs["L1_attn_bq"], np.float32),
        wk=np.ascontiguousarray(inputs["L1_attn_Wk"], np.float32),
        bk=np.ascontiguousarray(inputs["L1_attn_bk"], np.float32),
        wv=np.ascontiguousarray(inputs["L1_attn_Wv"], np.float32),
        bv=np.ascontiguousarray(inputs["L1_attn_bv"], np.float32),
        wo=np.ascontiguousarray(inputs["L1_attn_Wo"], np.float32),
        bo=np.ascontiguousarray(inputs["L1_attn_bo"], np.float32),
        ew=np.ascontiguousarray(inputs["L1_ffn_expert_w"], np.float32),
        eb=np.ascontiguousarray(inputs["L1_ffn_expert_b"], np.float32),
        cos_t=cos_t, sin_t=sin_t, sel16=sel16, sel16t=sel16t,
        sall=sall, **consts,
    )
    in_mapsB = [dict(x1=np.ascontiguousarray(x1[b]),
                     xn1=np.ascontiguousarray(xn1[b]),
                     sown=np.ascontiguousarray(s_T[b]), **wcomB)
                for b in range(NCORES)]
    resB = run_bass_kernel_spmd(ncB, in_mapsB, core_ids=list(range(NCORES)))
    out = np.stack([resB.results[b]["out"] for b in range(NCORES)], axis=0)
    return out.astype(np.float32)
